# version 02 (final)
# baseline (speedup 1.0000x reference)
import math
import contextlib
import numpy as np
import ml_dtypes
import concourse.bass as bass
import concourse.mybir as mybir
from concourse.bass_utils import run_bass_kernel_spmd

F32 = mybir.dt.float32
BF16 = mybir.dt.bfloat16
AF = mybir.ActivationFunctionType
ALU = mybir.AluOpType

B, L, D, HID = 4, 4096, 1024, 4096
NMODE = 32
M = 2 * L
N1, N2, N2H = 128, 64, 32
DC, HC = D // 128, HID // 128
DN = D * NMODE
DNC = DN // 128
NCORE = 8

M_TOK = 4096
KEEP = 2048
QT = 1024
GS = 8
KCH = 16
EPS = 1e-5

TWO_PI = 2.0 * math.pi
CW1 = 6.28125
CW2 = float(np.float32(np.float64(TWO_PI) - CW1))
CW3 = float(np.float32(np.float64(TWO_PI) - CW1 - np.float64(np.float32(np.float64(TWO_PI) - CW1))))
MAGIC = 12582912.0
INV2PI = 1.0 / TWO_PI

W12_COLS = HC * (DC * 128 * 2)
H1_COLS = HC * (HC * 128)
WBLOB_COLS = W12_COLS + 2 * H1_COLS + DC * (HC * 128)
OFF_W12, OFF_H1 = 0, W12_COLS
OFF_H2 = OFF_H1 + H1_COLS
OFF_W3 = OFF_H2 + H1_COLS
PROJ_COLS = DC * DC * 128

SC_S1, SC_S2, SC_SH1, SC_SH2, SC_S3, SC_B = 0, 32, 64, 96, 128, 136
SCL_COLS = 144
SS_LDT, SS_LAR, SS_AIM, SS_CR, SS_CI = 0, DNC, 2 * DNC, 3 * DNC, 4 * DNC
SSM_COLS = 5 * DNC
C_WA, C_EC, C_ES, C_NES, C_WB = 0, 128, 256, 384, 512
C_TWFC, C_TWFS, C_TWIC, C_TWIS = 528, 656, 784, 848
C_RAMPR, C_RAMPQ = 912, 1040
C_ONEC, C_ONER = 1072, 1073
C_IDF = 1074
CST_COLS = C_IDF + 128
DBG_COLS = 16384


def host_constants(core_id):
    n1 = np.arange(N1)
    k2 = np.arange(N2)
    cst = np.zeros((128, CST_COLS), np.float32)
    p = np.arange(N2H)[:, None]
    WA = np.concatenate([np.cos(TWO_PI * p * k2[None, :] / N2),
                         -np.sin(TWO_PI * p * k2[None, :] / N2)], axis=1)
    cst[:, C_WA:C_WA + 128] = np.tile(WA, (4, 1))
    cst[:, C_EC:C_EC + 128] = np.cos(TWO_PI * np.outer(n1, n1) / N1)
    cst[:, C_ES:C_ES + 128] = np.sin(TWO_PI * np.outer(n1, n1) / N1)
    cst[:, C_NES:C_NES + 128] = -np.sin(TWO_PI * np.outer(n1, n1) / N1)
    keep0 = (core_id % 2) * 16
    nn = (np.arange(16) + keep0)[None, :]
    jj = k2[:, None]
    cst[:, C_WB:C_WB + 16] = np.concatenate(
        [np.cos(TWO_PI * nn * jj / N2) / M, -np.sin(TWO_PI * nn * jj / N2) / M], axis=0)
    TWF_C = np.cos(TWO_PI * np.outer(k2, n1) / M)
    TWF_S = np.sin(TWO_PI * np.outer(k2, n1) / M)
    cst[0:64, C_TWFC:C_TWFC + 128] = TWF_C
    cst[0:64, C_TWFS:C_TWFS + 128] = TWF_S
    cst[:, C_TWIC:C_TWIC + 64] = TWF_C.T
    cst[:, C_TWIS:C_TWIS + 64] = TWF_S.T
    cst[:, C_RAMPR:C_RAMPR + 128] = np.tile(np.arange(N1, dtype=np.float32), (128, 1))
    cst[:, C_RAMPQ:C_RAMPQ + 32] = np.tile(128.0 * np.arange(N2H, dtype=np.float32), (128, 1))
    cst[:, C_ONEC] = 1.0
    cst[0, C_ONER] = 1.0
    cst[:, C_IDF:C_IDF + 128] = np.eye(128, dtype=np.float32)
    cstb = np.eye(128, dtype=ml_dtypes.bfloat16)
    return np.ascontiguousarray(cst), np.ascontiguousarray(cstb)


def _panels(w, n_po, n_pi):
    wT = np.ascontiguousarray(np.asarray(w, np.float32).T)
    return [np.ascontiguousarray(
        wT[:, po * 128:(po + 1) * 128].reshape(n_pi, 128, 128)
          .transpose(1, 0, 2).reshape(128, n_pi * 128))
        for po in range(n_po)]


def pack_ffn_weights(inp):
    p1 = _panels(inp["L0_ffn_w1_w"], HC, DC)
    p2 = _panels(inp["L0_ffn_w2_w"], HC, DC)
    ph1 = _panels(inp["L0_ffn_h1_w"], HC, HC)
    ph2 = _panels(inp["L0_ffn_h2_w"], HC, HC)
    p3 = _panels(inp["L0_ffn_w3_w"], DC, HC)
    seq = []
    for po in range(HC):
        seq.append(p1[po])
        seq.append(p2[po])
    seq += ph1 + ph2 + p3
    return np.ascontiguousarray(np.concatenate(seq, axis=1))


def pack_proj_weights(W):
    return np.ascontiguousarray(np.concatenate(_panels(W, DC, DC), axis=1))


def vec_cols(v):
    return np.ascontiguousarray(np.asarray(v, np.float32).reshape(-1, 128).T)


def dn_cols(a):
    return np.ascontiguousarray(np.asarray(a, np.float32).reshape(-1).reshape(-1, 128).T)


def pack_scales(inp):
    scl = np.zeros((128, SCL_COLS), np.float32)
    scl[:, SC_S1:SC_S1 + 32] = vec_cols(inp["L0_ffn_w1_s"])
    scl[:, SC_S2:SC_S2 + 32] = vec_cols(inp["L0_ffn_w2_s"])
    scl[:, SC_SH1:SC_SH1 + 32] = vec_cols(inp["L0_ffn_h1_s"])
    scl[:, SC_SH2:SC_SH2 + 32] = vec_cols(inp["L0_ffn_h2_s"])
    scl[:, SC_S3:SC_S3 + 8] = vec_cols(inp["L0_ffn_w3_s"])
    scl[:, SC_B:SC_B + 8] = vec_cols(inp["L0_attn_b"])
    return np.ascontiguousarray(scl)


def pack_ssm(inp):
    ld = np.asarray(inp["L0_attn_log_dt"], np.float32)
    ssm = np.zeros((128, SSM_COLS), np.float32)
    ssm[:, SS_LDT:SS_LDT + DNC] = dn_cols(np.repeat(ld[:, None], NMODE, axis=1))
    ssm[:, SS_LAR:SS_LAR + DNC] = dn_cols(inp["L0_attn_log_A_real"])
    ssm[:, SS_AIM:SS_AIM + DNC] = dn_cols(inp["L0_attn_A_imag"])
    ssm[:, SS_CR:SS_CR + DNC] = dn_cols(inp["L0_attn_C_real"])
    ssm[:, SS_CI:SS_CI + DNC] = dn_cols(inp["L0_attn_C_imag"])
    return np.ascontiguousarray(ssm)


class Sched:
    ENG = ("tensor", "vector", "scalar", "gpsimd", "sync")

    def __init__(self, nc, blk):
        self.nc, self.blk = nc, blk
        self.sem, self.dsem = {}, {}
        self.n = {k: 0 for k in self.ENG}
        self.nd = {k: 0 for k in self.ENG}

    def init_sems(self, ES):
        for k in self.ENG:
            self.sem[k] = ES.enter_context(self.nc.semaphore(f"st_{k}"))
            self.dsem[k] = ES.enter_context(self.nc.semaphore(f"sd_{k}"))

    def step(self, eng_name, deps, body, dma=False):
        deps = [d for d in deps if d is not None]
        sec = getattr(self.blk, eng_name)

        def _(eng):
            seen = {}
            for (kind, name, cnt) in deps:
                if kind == "c" and name == eng_name:
                    continue
                key = (kind, name)
                seen[key] = max(seen.get(key, 0), cnt)
            for (kind, name), cnt in seen.items():
                eng.wait_ge(self.sem[name] if kind == "c" else self.dsem[name], cnt)
            r = body(eng)
            if dma:
                self.nd[eng_name] += 16 * r
            else:
                r.then_inc(self.sem[eng_name], 1)
                self.n[eng_name] += 1
        sec(_)
        if dma:
            return ("d", eng_name, self.nd[eng_name])
        return ("c", eng_name, self.n[eng_name])

    def full_barrier(self):
        toks = []
        for k in self.ENG:
            if self.n[k]:
                toks.append(("c", k, self.n[k]))
            if self.nd[k]:
                toks.append(("d", k, self.nd[k]))
        return toks


def build_program(dump=(), phases=7):
    nc = bass.Bass("TRN2", target_bir_lowering=False)
    dump = set(dump)
    dbg_off = [0]
    dump_map = {}

    x_own = nc.dram_tensor("x_own", [D, M_TOK], F32, kind="ExternalInput").ap()
    wblob = nc.dram_tensor("wblob", [128, WBLOB_COLS], F32, kind="ExternalInput").ap()
    wproj = nc.dram_tensor("wproj", [128, PROJ_COLS], F32, kind="ExternalInput").ap()
    scl = nc.dram_tensor("scl", [128, SCL_COLS], F32, kind="ExternalInput").ap()
    ssmp = nc.dram_tensor("ssmp", [128, SSM_COLS], F32, kind="ExternalInput").ap()
    cst = nc.dram_tensor("cst", [128, CST_COLS], F32, kind="ExternalInput").ap()
    cstb = nc.dram_tensor("cstb", [128, 128], BF16, kind="ExternalInput").ap()
    out = nc.dram_tensor("out", [KEEP, D], F32, kind="ExternalOutput").ap()
    dbg = nc.dram_tensor("dbg", [128, DBG_COLS], F32, kind="ExternalOutput").ap()

    wbf = nc.dram_tensor("wbf", [128, WBLOB_COLS], BF16, kind="Internal").ap()
    wpb = nc.dram_tensor("wpb", [128, PROJ_COLS], BF16, kind="Internal").ap()
    u1g = nc.dram_tensor("u1g", [D, L], F32, kind="Internal").ap()
    u2g = nc.dram_tensor("u2g", [D, KEEP], F32, kind="Internal").ap()
    ukg = nc.dram_tensor("ukg", [D, KEEP], F32, kind="Internal").ap()
    u3g = nc.dram_tensor("u3g", [D, KEEP], F32, kind="Internal").ap()
    xh2g = nc.dram_tensor("xh2g", [D, L], BF16, kind="Internal").ap()
    gcg = nc.dram_tensor("gcg", [D, KEEP], BF16, kind="Internal").ap()
    hhat = nc.dram_tensor("hhat", [D, 2 * N1 * N2], BF16, kind="Internal").ap()

    pos = [20480]
    offs = {}

    def alloc(name, shape, dtype, at=None):
        nbytes = shape[1] * mybir.dt.size(dtype)
        if at is None:
            at = pos[0]
            pos[0] = at + ((nbytes + 31) // 32) * 32
        offs[name] = at
        return nc.alloc_sbuf_tensor_at(name, shape, dtype, offset=at)

    c_wa = alloc("c_wa", [128, 128], BF16)
    c_ec = alloc("c_ec", [128, 128], BF16)
    c_es = alloc("c_es", [128, 128], BF16)
    c_nes = alloc("c_nes", [128, 128], BF16)
    c_wb = alloc("c_wb", [128, 16], BF16)
    c_id = alloc("c_id", [128, 128], BF16)
    c_f32 = alloc("c_f32", [128, CST_COLS - C_TWFC], F32)
    sc_sb = alloc("sc_sb", [128, SCL_COLS + 8], F32)
    res_b = alloc("res_b", [128, 2 * QT], F32)
    statm = alloc("statm", [1, QT], F32)
    statv = alloc("statv", [1, QT], F32)
    statr = alloc("statr", [1, QT], F32)
    osb = alloc("osb", [128, QT], F32)
    hsil = alloc("hsil", [128, QT], F32, at=offs["osb"])
    u1t = alloc("u1t", [128, QT], F32)
    xst = alloc("xst", [128, QT], BF16)
    P0 = pos[0]
    xh_q = alloc("xh_q", [128, DC * QT], BF16)
    h_q = alloc("h_q", [128, HC * QT], BF16)
    g_q = alloc("g_q", [128, HC * QT], BF16)
    wbuf = alloc("wbuf", [128, 2 * 4096], BF16)
    assert pos[0] <= 229376, pos[0]
    cbase = P0
    CSET = 36864
    zt, tpk, tts_a, tts_b, hhs, pps, hps, gcb = [], [], [], [], [], [], [], []
    tw1, tw2, fq1, fq2, ti1, ti2 = [], [], [], [], [], []
    for p_ in (0, 1):
        b = cbase + p_ * CSET
        zt.append(alloc(f"zt{p_}", [32, GS * 128], BF16, at=b))
        tpk.append(alloc(f"tpk{p_}", [128, GS * 128], BF16, at=b + 2048))
        tts_a.append(alloc(f"ttsa{p_}", [128, GS * 128], BF16, at=b + 4096))
        tts_b.append(alloc(f"ttsb{p_}", [128, GS * 128], BF16, at=b + 6144))
        hhs.append(alloc(f"hhs{p_}", [128, 2 * GS * 64], BF16, at=b + 8192))
        pps.append(alloc(f"pps{p_}", [128, 2 * GS * 64], BF16, at=b + 10240))
        hps.append(alloc(f"hps{p_}", [128, GS * 128], BF16, at=b + 12288))
        gcb.append(alloc(f"gcb{p_}", [16, GS * 128], BF16, at=b + 14336))
        tw1.append(alloc(f"tw1{p_}", [64, GS * 128], F32, at=b + 16384))
        tw2.append(alloc(f"tw2{p_}", [64, GS * 128], F32, at=b + 20480))
        fq1.append(alloc(f"fq1{p_}", [128, GS * 64], F32, at=b + 24576))
        fq2.append(alloc(f"fq2{p_}", [128, GS * 64], F32, at=b + 26624))
        ti1.append(alloc(f"ti1{p_}", [128, GS * 64], F32, at=b + 28672))
        ti2.append(alloc(f"ti2{p_}", [128, GS * 64], F32, at=b + 30720))
    thR = alloc("thR", [128, DNC * 128], BF16, at=P0)
    thRi = alloc("thRi", [128, DNC * 128], BF16, at=P0 + 65536)
    thQ = alloc("thQ", [128, DNC * 32], BF16, at=P0 + 131072)
    thQi = alloc("thQi", [128, DNC * 32], BF16, at=P0 + 147456)
    kg = alloc("kg", [128, 12 * DNC], F32, at=offs["res_b"])
    big = alloc("big", [128, 4096], F32, at=offs["statv"])
    kz = alloc("kz", [128, 512], BF16, at=P0 + 163840)
    ktp = alloc("ktp", [128, 1024], BF16, at=P0 + 164864)
    ktt = alloc("ktt", [128, 1024], BF16, at=P0 + 166912)
    khh = alloc("khh", [128, 2048], BF16, at=P0 + 168960)
    kz2 = alloc("kz2", [32, 2048], BF16, at=offs["osb"])
    assert P0 + 171008 <= 229376 and pos[0] <= 229376, (P0, pos[0])
    ssm_sb = alloc("ssm_sb", [128, SSM_COLS], F32, at=cbase)

    with contextlib.ExitStack() as ES:
        PA = ES.enter_context(nc.psum_tensor("PA", [128, 2048], F32))
        PB = ES.enter_context(nc.psum_tensor("PB", [128, 1024], F32))
        PS = ES.enter_context(nc.psum_tensor("PS", [128, 512], F32))
        PT = ES.enter_context(nc.psum_tensor("PT", [128, 1024], BF16))
        BANKS = [PA[:, 0:512], PA[:, 512:1024], PA[:, 1024:1536], PA[:, 1536:2048],
                 PB[:, 0:512], PB[:, 512:1024]]

        blk = ES.enter_context(nc.Block())
        S = Sched(nc, blk)
        S.init_sems(ES)

        def cf(col, w=1, rows=128):
            return c_f32[0:rows, col - C_TWFC:col - C_TWFC + w]

        def dump_hbm(tag, src_ap, rows, cols):
            if tag not in dump:
                return
            off = dbg_off[0]
            assert off + cols <= DBG_COLS
            dbg_off[0] += cols
            dump_map[tag] = (off, rows, cols)

            def body(g):
                g.dma_start(dbg[0:rows, off:off + cols], src_ap).then_inc(S.dsem["gpsimd"], 16)
                return 1
            S.step("gpsimd", S.full_barrier(), body, dma=True)

        def finish():
            @blk.sync
            def _(sy):
                for (kind, name, cnt) in S.full_barrier():
                    sy.wait_ge(S.sem[name] if kind == "c" else S.dsem[name], cnt)
            return nc, dump_map

        def b_g(g):
            g.dma_start(c_wa[:], cst[:, C_WA:C_WA + 128]).then_inc(S.dsem["gpsimd"], 16)
            g.dma_start(c_ec[:], cst[:, C_EC:C_EC + 128]).then_inc(S.dsem["gpsimd"], 16)
            g.dma_start(c_es[:], cst[:, C_ES:C_ES + 128]).then_inc(S.dsem["gpsimd"], 16)
            g.dma_start(c_nes[:], cst[:, C_NES:C_NES + 128]).then_inc(S.dsem["gpsimd"], 16)
            g.dma_start(c_wb[:], cst[:, C_WB:C_WB + 16]).then_inc(S.dsem["gpsimd"], 16)
            return 5
        tok_bg = S.step("gpsimd", [], b_g, dma=True)

        def b_s(sy):
            sy.dma_start(c_id[:], cstb[:, 0:128]).then_inc(S.dsem["sync"], 16)
            sy.dma_start(c_f32[:], cst[:, C_TWFC:]).then_inc(S.dsem["sync"], 16)
            sy.dma_start(sc_sb[:, 0:SCL_COLS], scl[:, :]).then_inc(S.dsem["sync"], 16)
            sy.dma_start(ssm_sb[:], ssmp[:, :]).then_inc(S.dsem["sync"], 16)
            return 4
        tok_bs = S.step("sync", [], b_s, dma=True)

        NWC = WBLOB_COLS // 4096
        NPC = PROJ_COLS // 4096

        def b_wc(g):
            for c in range(NWC):
                g.dma_start(wbf[:, c * 4096:(c + 1) * 4096],
                            wblob[:, c * 4096:(c + 1) * 4096]).then_inc(S.dsem["gpsimd"], 16)
            for c in range(NPC):
                g.dma_start(wpb[:, c * 4096:(c + 1) * 4096],
                            wproj[:, c * 4096:(c + 1) * 4096]).then_inc(S.dsem["gpsimd"], 16)
            return NWC + NPC
        tok_wc = S.step("gpsimd", [], b_wc, dma=True)

        def b_s3h(v):
            return v.tensor_scalar(sc_sb[:, SCL_COLS:SCL_COLS + 8],
                                   sc_sb[:, SC_S3:SC_S3 + 8], 0.5, None, ALU.mult)
        tok_s3h = S.step("vector", [tok_bs], b_s3h)
        boot = [tok_bg, tok_bs, tok_s3h]
        if phases < 1:
            return finish()

        _sc_kg = nc.enter_named_scope("kgrids", False)
        def gsl(i):
            return kg[:, i * DNC:(i + 1) * DNC]

        def k1f(s):
            s.activation(gsl(0), ssm_sb[:, SS_LDT:SS_LDT + DNC], AF.Exp)
            return s.activation(gsl(1), ssm_sb[:, SS_LAR:SS_LAR + DNC], AF.Exp)
        tk = S.step("scalar", [tok_bs], k1f)

        def k2f(v):
            v.tensor_tensor(gsl(2), gsl(1), gsl(0), ALU.mult)
            v.tensor_scalar(gsl(2), gsl(2), -1.0, None, ALU.mult)
            v.tensor_tensor(gsl(3), ssm_sb[:, SS_AIM:SS_AIM + DNC], gsl(0), ALU.mult)
            v.tensor_scalar(gsl(8), gsl(3), INV2PI, MAGIC, ALU.mult, ALU.add)
            v.tensor_scalar(gsl(8), gsl(8), -MAGIC, None, ALU.add)
            v.scalar_tensor_tensor(gsl(9), gsl(8), -CW1, gsl(3), ALU.mult, ALU.add)
            v.scalar_tensor_tensor(gsl(9), gsl(8), -CW2, gsl(9), ALU.mult, ALU.add)
            return v.scalar_tensor_tensor(gsl(9), gsl(8), -CW3, gsl(9), ALU.mult, ALU.add)
        tk = S.step("vector", [tk], k2f)

        def k3f(s):
            s.activation(gsl(5), gsl(9), AF.Sin)
            s.activation(gsl(6), gsl(9), AF.Sin, scale=0.5)
            return s.activation(gsl(4), gsl(2), AF.Exp)
        tk = S.step("scalar", [tk], k3f)

        def k4f(v):
            Aim = ssm_sb[:, SS_AIM:SS_AIM + DNC]
            Cr = ssm_sb[:, SS_CR:SS_CR + DNC]
            Ci = ssm_sb[:, SS_CI:SS_CI + DNC]
            v.tensor_tensor(gsl(6), gsl(6), gsl(6), ALU.mult)
            v.tensor_scalar(gsl(6), gsl(6), -2.0, 1.0, ALU.mult, ALU.add)
            v.tensor_tensor(gsl(7), gsl(4), gsl(6), ALU.mult)
            v.tensor_scalar(gsl(7), gsl(7), 1.0, -1.0, ALU.mult, ALU.add)
            v.tensor_tensor(gsl(8), gsl(4), gsl(5), ALU.mult)
            v.tensor_tensor(gsl(9), gsl(1), gsl(1), ALU.mult)
            v.tensor_tensor(gsl(6), Aim, Aim, ALU.mult)
            v.tensor_tensor(gsl(9), gsl(9), gsl(6), ALU.add)
            v.reciprocal(gsl(9), gsl(9))
            v.tensor_tensor(gsl(4), gsl(7), gsl(1), ALU.mult)
            v.tensor_tensor(gsl(5), gsl(8), Aim, ALU.mult)
            v.tensor_tensor(gsl(4), gsl(5), gsl(4), ALU.subtract)
            v.tensor_tensor(gsl(4), gsl(4), gsl(9), ALU.mult)
            v.tensor_tensor(gsl(5), gsl(8), gsl(1), ALU.mult)
            v.tensor_tensor(gsl(6), gsl(7), Aim, ALU.mult)
            v.tensor_tensor(gsl(5), gsl(5), gsl(6), ALU.add)
            v.tensor_scalar(gsl(5), gsl(5), -1.0, None, ALU.mult)
            v.tensor_tensor(gsl(5), gsl(5), gsl(9), ALU.mult)
            v.tensor_tensor(gsl(10), Cr, gsl(4), ALU.mult)
            v.tensor_tensor(gsl(6), Ci, gsl(5), ALU.mult)
            v.tensor_tensor(gsl(10), gsl(10), gsl(6), ALU.subtract)
            v.tensor_tensor(gsl(11), Cr, gsl(5), ALU.mult)
            v.tensor_tensor(gsl(6), Ci, gsl(4), ALU.mult)
            v.tensor_tensor(gsl(11), gsl(11), gsl(6), ALU.add)
            v.tensor_tensor(gsl(4), gsl(10), gsl(10), ALU.mult)
            v.tensor_tensor(gsl(6), gsl(11), gsl(11), ALU.mult)
            v.tensor_tensor(gsl(4), gsl(4), gsl(6), ALU.add)
            v.tensor_scalar(gsl(4), gsl(4), 1.0, 1e-38, ALU.mult, ALU.add)
            v.reciprocal(gsl(6), gsl(10))
            return v.tensor_tensor(gsl(5), gsl(11), gsl(6), ALU.mult)
        tk = S.step("vector", [tk], k4f)

        def k5f(s):
            s.activation(gsl(4), gsl(4), AF.Ln)
            return s.activation(gsl(5), gsl(5), AF.Arctan)
        tk = S.step("scalar", [tk], k5f)

        def k6f(v):
            v.tensor_scalar(gsl(0), gsl(4), 0.5, math.log(2.0), ALU.mult, ALU.add)
            v.tensor_scalar(gsl(6), gsl(10), 0.0, None, ALU.is_lt)
            v.tensor_scalar(gsl(7), gsl(11), 0.0, None, ALU.is_lt)
            v.tensor_scalar(gsl(7), gsl(7), -2.0, 1.0, ALU.mult, ALU.add)
            v.tensor_tensor(gsl(6), gsl(6), gsl(7), ALU.mult)
            v.tensor_scalar(gsl(6), gsl(6), math.pi, None, ALU.mult)
            return v.tensor_tensor(gsl(1), gsl(5), gsl(6), ALU.add)
        tk = S.step("vector", [tk], k6f)

        rampR = cf(C_RAMPR, 128)
        rampQ = cf(C_RAMPQ, 32)
        CB = 8
        for bi_ in range(DNC // CB):
            c0 = bi_ * CB

            def rv1(v, c0=c0):
                for c in range(c0, c0 + CB):
                    j = (c - c0) * 128
                    v.tensor_scalar(big[:, j:j + 128], rampR, gsl(2)[:, c:c + 1], None, ALU.mult)
                    v.tensor_scalar(big[:, 2048 + j:2176 + j], rampR, gsl(3)[:, c:c + 1], None, ALU.mult)
                im, k = big[:, 2048:3072], big[:, 3072:4096]
                v.tensor_scalar(k, im, INV2PI, MAGIC, ALU.mult, ALU.add)
                v.tensor_scalar(k, k, -MAGIC, None, ALU.add)
                v.scalar_tensor_tensor(im, k, -CW1, im, ALU.mult, ALU.add)
                v.scalar_tensor_tensor(im, k, -CW2, im, ALU.mult, ALU.add)
                return v.scalar_tensor_tensor(im, k, -CW3, im, ALU.mult, ALU.add)
            tk = S.step("vector", [tk, tok_bs], rv1)

            def rs1(s):
                s.activation(big[:, 0:1024], big[:, 0:1024], AF.Exp)
                s.activation(big[:, 3072:4096], big[:, 2048:3072], AF.Sin, scale=0.5)
                return s.activation(big[:, 2048:3072], big[:, 2048:3072], AF.Sin)
            tk = S.step("scalar", [tk], rs1)

            def rv2(v, c0=c0):
                h = big[:, 3072:4096]
                v.tensor_tensor(h, h, h, ALU.mult)
                v.tensor_scalar(h, h, -2.0, 1.0, ALU.mult, ALU.add)
                o = c0 * 128
                v.tensor_tensor(thR[:, o:o + 1024], big[:, 0:1024], h, ALU.mult)
                v.tensor_tensor(h, big[:, 0:1024], big[:, 2048:3072], ALU.mult)
                return v.tensor_scalar(thRi[:, o:o + 1024], h, -1.0, None, ALU.mult)
            tk = S.step("vector", [tk], rv2)

            def qv1(v, c0=c0):
                for c in range(c0, c0 + CB):
                    j = (c - c0) * 32
                    v.tensor_scalar(big[:, j:j + 32], rampQ, gsl(2)[:, c:c + 1],
                                    gsl(0)[:, c:c + 1], ALU.mult, ALU.add)
                    v.tensor_scalar(big[:, 256 + j:288 + j], rampQ, gsl(3)[:, c:c + 1],
                                    gsl(1)[:, c:c + 1], ALU.mult, ALU.add)
                im, k = big[:, 256:512], big[:, 512:768]
                v.tensor_scalar(k, im, INV2PI, MAGIC, ALU.mult, ALU.add)
                v.tensor_scalar(k, k, -MAGIC, None, ALU.add)
                v.scalar_tensor_tensor(im, k, -CW1, im, ALU.mult, ALU.add)
                v.scalar_tensor_tensor(im, k, -CW2, im, ALU.mult, ALU.add)
                return v.scalar_tensor_tensor(im, k, -CW3, im, ALU.mult, ALU.add)
            tk = S.step("vector", [tk], qv1)

            def qs1(s):
                s.activation(big[:, 0:256], big[:, 0:256], AF.Exp)
                s.activation(big[:, 512:768], big[:, 256:512], AF.Sin, scale=0.5)
                return s.activation(big[:, 256:512], big[:, 256:512], AF.Sin)
            tk = S.step("scalar", [tk], qs1)

            def qv2(v, c0=c0):
                h = big[:, 512:768]
                v.tensor_tensor(h, h, h, ALU.mult)
                v.tensor_scalar(h, h, -2.0, 1.0, ALU.mult, ALU.add)
                o = c0 * 32
                v.tensor_tensor(thQ[:, o:o + 256], big[:, 0:256], h, ALU.mult)
                return v.tensor_tensor(thQi[:, o:o + 256], big[:, 0:256], big[:, 256:512], ALU.mult)
            tk = S.step("vector", [tk], qv2)
        nc.leave_named_scope("kgrids", _sc_kg[0], False)
        grids_done = tk
        if dump and "burst" in dump:
            _sc_b = nc.enter_named_scope("burst", False)

            def bmm(t):
                ins = None
                for c in range(512):
                    for pi in range(8):
                        ins = t.matmul(PA[:, 0:512], wbuf[:, pi * 128:(pi + 1) * 128],
                                       xh_q[:, 0:512], start=(pi == 0), stop=(pi == 7))
                return ins
            grids_done = S.step("tensor", [grids_done, tok_wc], bmm)
            nc.leave_named_scope("burst", _sc_b[0], False)
        if phases < 2:
            return finish()

        _sc_kr = nc.enter_named_scope("krounds", False)
        prev_round = grids_done
        for rnd in range(D // KCH):
            c0k = rnd * 4

            def kmm(t, c0k=c0k):
                for ci in range(4):
                    c = c0k + ci
                    for g4 in range(4):
                        lr = thQ[32 * g4:32 * g4 + 32, c * 32:(c + 1) * 32]
                        li = thQi[32 * g4:32 * g4 + 32, c * 32:(c + 1) * 32]
                        rr = thR[32 * g4:32 * g4 + 32, c * 128:(c + 1) * 128]
                        ri = thRi[32 * g4:32 * g4 + 32, c * 128:(c + 1) * 128]
                        t.matmul(PS[32 * g4:32 * g4 + 32, ci * 128:(ci + 1) * 128], lr, rr,
                                 start=True, stop=False, tile_position=(32 * g4, 32 * g4))
                        ins = t.matmul(PS[32 * g4:32 * g4 + 32, ci * 128:(ci + 1) * 128], li, ri,
                                       start=False, stop=True, tile_position=(32 * g4, 32 * g4))
                return ins
            tkm = S.step("tensor", [prev_round, grids_done, tok_bg], kmm)

            def kev(s):
                return s.copy(kz[:, 0:512], PS[:, 0:512])
            tke = S.step("scalar", [tkm], kev)

            def krel(g):
                for g4 in range(4):
                    g.dma_start(kz2[0:32, g4 * 512:(g4 + 1) * 512],
                                kz[32 * g4:32 * g4 + 32, 0:512]).then_inc(S.dsem["gpsimd"], 16)
                return 4
            tke = S.step("gpsimd", [tke], krel, dma=True)

            prev_half = None
            half_toks = []
            for hf in range(2):
                def kfA(t, hf=hf):
                    for ch in range(8):
                        cc = hf * 8 + ch
                        ci, g4 = divmod(cc, 4)
                        rhs = kz2[0:32, g4 * 512 + ci * 128: g4 * 512 + (ci + 1) * 128]
                        ins = t.matmul(PA[:, ch * 128:(ch + 1) * 128],
                                       c_wa[0:32, :], rhs, start=True, stop=True)
                    return ins
                tfa = S.step("tensor", [tke, prev_half], kfA)

                def ktw(v):
                    tc = cf(C_TWFC, 128, 64)[:, None, :].broadcast_to((64, 8, 128))
                    ts_ = cf(C_TWFS, 128, 64)[:, None, :].broadcast_to((64, 8, 128))
                    Gr = PA[0:64, 0:1024].rearrange("p (s f) -> p s f", s=8)
                    Gi = PA[64:128, 0:1024].rearrange("p (s f) -> p s f", s=8)
                    w1 = big[0:64, 0:1024].rearrange("p (s f) -> p s f", s=8)
                    w2 = big[0:64, 1024:2048].rearrange("p (s f) -> p s f", s=8)
                    Tr = ktp[0:64, :].rearrange("p (s f) -> p s f", s=8)
                    Ti = ktp[64:128, :].rearrange("p (s f) -> p s f", s=8)
                    v.tensor_tensor(w1, Gr, tc, ALU.mult)
                    v.tensor_tensor(w2, Gi, ts_, ALU.mult)
                    v.tensor_tensor(Tr, w1, w2, ALU.add)
                    v.tensor_tensor(w1, Gi, tc, ALU.mult)
                    v.tensor_tensor(w2, Gr, ts_, ALU.mult)
                    return v.tensor_tensor(Ti, w1, w2, ALU.subtract)
                ttw = S.step("vector", [tfa], ktw)

                def ktr(t):
                    for s8 in range(8):
                        ins = t.transpose(PT[:, s8 * 128:(s8 + 1) * 128],
                                          ktp[:, s8 * 128:(s8 + 1) * 128], c_id[:])
                    return ins
                ttr = S.step("tensor", [ttw], ktr)

                def kev2(s):
                    return s.copy(ktt[:, 0:1024], PT[:, 0:1024])
                tev2 = S.step("scalar", [ttr], kev2)

                def kfB(t):
                    TTr = ktt[:, 0:1024].rearrange("p (s f) -> p s f", s=8)[:, :, 0:64]
                    TTi = ktt[:, 0:1024].rearrange("p (s f) -> p s f", s=8)[:, :, 64:128]
                    t.matmul(PB[:, 0:512], c_ec[:], TTr, start=True, stop=False)
                    t.matmul(PB[:, 0:512], c_es[:], TTi, start=False, stop=True)
                    t.matmul(PB[:, 512:1024], c_ec[:], TTi, start=True, stop=False)
                    return t.matmul(PB[:, 512:1024], c_nes[:], TTr, start=False, stop=True)
                tfb = S.step("tensor", [tev2], kfB)

                def kev3(s, hf=hf):
                    s.copy(khh[:, hf * 1024:hf * 1024 + 512], PB[:, 0:512])
                    return s.copy(khh[:, hf * 1024 + 512:hf * 1024 + 1024], PB[:, 512:1024])
                prev_half = S.step("scalar", [tfb], kev3)
                half_toks.append(prev_half)

            def khd(g, rnd=rnd):
                d0 = rnd * KCH
                for hf in range(2):
                    base = hf * 1024
                    src_r = khh[:, base:base + 512].rearrange("p (s f) -> p s f", s=8)
                    src_i = khh[:, base + 512:base + 1024].rearrange("p (s f) -> p s f", s=8)
                    dst_r = hhat[d0 + hf * 8:d0 + hf * 8 + 8, 0:8192].rearrange("s (p f) -> p s f", p=128)
                    dst_i = hhat[d0 + hf * 8:d0 + hf * 8 + 8, 8192:16384].rearrange("s (p f) -> p s f", p=128)
                    g.dma_start(dst_r, src_r).then_inc(S.dsem["gpsimd"], 16)
                    g.dma_start(dst_i, src_i).then_inc(S.dsem["gpsimd"], 16)
                return 4
            prev_round = S.step("gpsimd", half_toks, khd, dma=True)
        nc.leave_named_scope("krounds", _sc_kr[0], False)
        kprep_done = prev_round
        dump_hbm("hhat", hhat[0:128, 0:2048], 128, 2048)
        if phases < 3:
            dump_hbm("kg", kg[:, 0:2048], 128, 2048)
            dump_hbm("thR", thR[:, 0:1024], 128, 1024)
            dump_hbm("thRi", thRi[:, 0:1024], 128, 1024)
            dump_hbm("thQ", thQ[:, 0:256], 128, 256)
            dump_hbm("thQi", thQi[:, 0:256], 128, 256)
            dump_hbm("kz2", kz2[0:32, 0:2048], 32, 2048)
            return finish()

        bank_idx = [0]
        bank_users = {}

        def next_bank(nb=6):
            i = bank_idx[0] % nb
            bank_idx[0] += 1
            return i

        def emit_ln2(dep, src_hbm, col0, sink, post_ch=None):
            stats_toks = {}
            for ch in range(DC):
                slot = ch % 2

                def ld(g, ch=ch, slot=slot):
                    g.dma_start(res_b[:, slot * QT:slot * QT + QT],
                                src_hbm[ch * 128:(ch + 1) * 128, col0:col0 + QT]
                                ).then_inc(S.dsem["gpsimd"], 16)
                    return 1
                dep_l = list(dep) if ch < 2 else [stats_toks[ch - 2]]
                tok_l = S.step("gpsimd", dep_l, ld, dma=True)

                def sq(s, ch=ch, slot=slot):
                    return s.activation(hsil[:, 0:QT], res_b[:, slot * QT:slot * QT + QT],
                                        AF.Square)
                dep_sq = [tok_l] + ([stats_toks[ch - 1]] if ch >= 1 else [])
                tok_sq = S.step("scalar", dep_sq, sq)

                def sm(t, ch=ch, slot=slot):
                    onec = cf(C_ONEC, 1)
                    for hh in range(2):
                        t.matmul(PB[0:1, hh * 512:hh * 512 + 512], onec,
                                 res_b[:, slot * QT + hh * 512:slot * QT + hh * 512 + 512],
                                 start=(ch == 0), stop=(ch == DC - 1))
                    for hh in range(2):
                        ins = t.matmul(PA[0:1, hh * 512:hh * 512 + 512], onec,
                                       hsil[:, hh * 512:hh * 512 + 512],
                                       start=(ch == 0), stop=(ch == DC - 1))
                    return ins
                tok_sm = S.step("tensor", [tok_sq] + ([stats_toks[ch - 1]] if ch >= 1 else list(dep)), sm)
                stats_toks[ch] = tok_sm

            def lnv(v):
                v.tensor_scalar(statm[0:1, 0:QT], PB[0:1, 0:1024], 1.0 / D, None, ALU.mult)
                v.tensor_scalar(statv[0:1, 0:QT], PA[0:1, 0:1024], 1.0 / D, EPS, ALU.mult, ALU.add)
                v.tensor_tensor(statr[0:1, 0:QT], statm[0:1, 0:QT], statm[0:1, 0:QT], ALU.mult)
                return v.tensor_tensor(statv[0:1, 0:QT], statv[0:1, 0:QT], statr[0:1, 0:QT],
                                       ALU.subtract)
            tok_v = S.step("vector", [stats_toks[DC - 1]], lnv)

            def lnr(s):
                s.activation(statv[0:1, 0:QT], statv[0:1, 0:QT], AF.Ln)
                return s.activation(statr[0:1, 0:QT], statv[0:1, 0:QT], AF.Exp, scale=-0.5)
            tok_r = S.step("scalar", [tok_v], lnr)

            def bc(t):
                oner = cf(C_TWFC, 128, 1)
                t.matmul(PS[:, 0:512], oner, statr[0:1, 0:512], start=True, stop=True)
                t.matmul(PA[:, 0:512], oner, statr[0:1, 512:1024], start=True, stop=True)
                t.matmul(PA[:, 1024:1536], oner, statm[0:1, 0:512], start=True, stop=True)
                return t.matmul(PA[:, 1536:2048], oner, statm[0:1, 512:1024], start=True, stop=True)
            tok_b = S.step("tensor", [tok_r], bc)

            norm_toks = []
            post_toks = []
            for ch in range(DC):
                slot = ch % 2

                def ld2(g, ch=ch, slot=slot):
                    g.dma_start(res_b[:, slot * QT:slot * QT + QT],
                                src_hbm[ch * 128:(ch + 1) * 128, col0:col0 + QT]
                                ).then_inc(S.dsem["gpsimd"], 16)
                    return 1
                dep_l = [tok_b] if ch < 2 else [norm_toks[ch - 2]]
                tok_l = S.step("gpsimd", dep_l, ld2, dma=True)

                def nm(v, ch=ch, slot=slot):
                    for hh in range(2):
                        rb = PS[:, 0:512] if hh == 0 else PA[:, 0:512]
                        mb = PA[:, 1024:1536] if hh == 0 else PA[:, 1536:2048]
                        sl = slice(slot * QT + hh * 512, slot * QT + hh * 512 + 512)
                        v.tensor_tensor(osb[:, hh * 512:hh * 512 + 512], res_b[:, sl], mb,
                                        ALU.subtract)
                        ins = sink(ch, hh, v, osb[:, hh * 512:hh * 512 + 512], rb)
                    return ins
                dep_n = [tok_l, tok_b] + (post_toks[-1:] if post_toks else [])
                tok_n = S.step("vector", dep_n, nm)
                norm_toks.append(tok_n)
                if post_ch is not None:
                    post_toks.append(post_ch(ch, tok_n))
            return norm_toks + post_toks

        def emit_pass(dep, n_po, n_pi, woff, panel_cols, banks_sel, src_fn, out_fn,
                      extra_dma_fn=None, wsrc=None):
            mm_by_slot = [None, None]
            ep_all = []
            wsrc = wsrc if wsrc is not None else wbf
            for po in range(n_po):
                slot = po % 2

                def pd(g, po=po, slot=slot):
                    g.dma_start(wbuf[:, slot * 4096:slot * 4096 + panel_cols],
                                wsrc[:, woff + po * panel_cols: woff + (po + 1) * panel_cols]
                                ).then_inc(S.dsem["gpsimd"], 16)
                    n = 1
                    if extra_dma_fn is not None:
                        n += extra_dma_fn(g, po, slot)
                    return n
                dep_d = (list(dep) + [tok_wc]) if po < 2 else [mm_by_slot[slot]]
                tok_d = S.step("gpsimd", dep_d, pd, dma=True)

                for hh in range(2):
                    bi = banks_sel[next_bank(len(banks_sel))]
                    bk = BANKS[bi]

                    def mm(t, po=po, slot=slot, hh=hh, bk=bk):
                        for pi in range(n_pi):
                            lhsT = wbuf[:, slot * 4096 + pi * 128: slot * 4096 + pi * 128 + 128]
                            r = src_fn(pi, hh)
                            ins = t.matmul(bk, lhsT, r, start=(pi == 0), stop=(pi == n_pi - 1))
                        return ins
                    dep_m = [tok_d] + list(dep) + [bank_users.get(bi)]
                    tok_m = S.step("tensor", dep_m, mm)
                    if hh == 1:
                        mm_by_slot[slot] = tok_m
                    tok_e = out_fn(po, hh, bk, tok_m)
                    bank_users[bi] = tok_e
                    ep_all.append(tok_e)
            return ep_all

        def emit_ffn_app(dep, app, T, src_res_hbm, u_out_hbm):
            nq = T // QT
            prev_q = list(dep)
            for tq in range(nq):
                col0 = tq * QT

                def sink_xh(ch, hh, v, t_ap, rb):
                    return v.tensor_tensor(
                        xh_q[:, ch * QT + hh * 512:ch * QT + hh * 512 + 512], t_ap, rb, ALU.mult)
                ln_all = emit_ln2(prev_q, src_res_hbm, col0, sink_xh)
                tok_n = ln_all[-1]

                mm_by_slot = [None, None]
                hsil_user = {}
                last_h = None
                for po in range(HC):
                    slot = po % 2

                    def pd(g, po=po, slot=slot):
                        g.dma_start(wbuf[:, slot * 4096:slot * 4096 + 2048],
                                    wbf[:, OFF_W12 + po * 2048: OFF_W12 + (po + 1) * 2048]
                                    ).then_inc(S.dsem["gpsimd"], 16)
                        return 1
                    dep_d = ([tok_n, tok_wc] if po < 2 else [mm_by_slot[slot]])
                    tok_d = S.step("gpsimd", dep_d, pd, dma=True)

                    for hh in range(2):
                        b1 = next_bank()
                        b2 = next_bank()
                        bk1, bk2 = BANKS[b1], BANKS[b2]

                        def mm(t, po=po, slot=slot, hh=hh, bk1=bk1, bk2=bk2):
                            for pi in range(DC):
                                l1 = wbuf[:, slot * 4096 + pi * 128: slot * 4096 + pi * 128 + 128]
                                l2 = wbuf[:, slot * 4096 + 1024 + pi * 128:
                                          slot * 4096 + 1024 + pi * 128 + 128]
                                r = xh_q[:, pi * QT + hh * 512: pi * QT + hh * 512 + 512]
                                t.matmul(bk1, l1, r, start=(pi == 0), stop=(pi == DC - 1))
                                ins = t.matmul(bk2, l2, r, start=(pi == 0), stop=(pi == DC - 1))
                            return ins
                        dep_m = [tok_d] + ln_all + [bank_users.get(b1), bank_users.get(b2)]
                        tok_m = S.step("tensor", dep_m, mm)
                        if hh == 1:
                            mm_by_slot[slot] = tok_m
                        par = (po * 2 + hh) % 2

                        def si(s, po=po, bk1=bk1, par=par):
                            return s.activation(hsil[:, par * 512:par * 512 + 512], bk1, AF.Silu,
                                                scale=sc_sb[:, SC_S1 + po:SC_S1 + po + 1])
                        tok_si = S.step("scalar", [tok_m, hsil_user.get(par)], si)

                        def hm(v, po=po, hh=hh, bk2=bk2, par=par):
                            return v.scalar_tensor_tensor(
                                h_q[:, po * QT + hh * 512: po * QT + hh * 512 + 512],
                                bk2, sc_sb[:, SC_S2 + po:SC_S2 + po + 1],
                                hsil[:, par * 512:par * 512 + 512], ALU.mult, ALU.mult)
                        tok_e = S.step("vector", [tok_si], hm)
                        hsil_user[par] = tok_e
                        bank_users[b1] = tok_e
                        bank_users[b2] = tok_e
                        last_h = tok_e

                def h1_src(pi, hh):
                    return h_q[:, pi * QT + hh * 512: pi * QT + hh * 512 + 512]

                def h1_out(po, hh, bk, tok_m):
                    def ga(s, po=po, hh=hh, bk=bk):
                        return s.activation(g_q[:, po * QT + hh * 512: po * QT + hh * 512 + 512],
                                            bk, AF.Gelu,
                                            scale=sc_sb[:, SC_SH1 + po:SC_SH1 + po + 1])
                    return S.step("scalar", [tok_m], ga)
                eps1 = emit_pass([last_h], HC, HC, OFF_H1, 4096, list(range(6)), h1_src, h1_out)

                def h2_src(pi, hh):
                    return g_q[:, pi * QT + hh * 512: pi * QT + hh * 512 + 512]

                def h2_out(po, hh, bk, tok_m):
                    def ga(s, po=po, hh=hh, bk=bk):
                        return s.activation(h_q[:, po * QT + hh * 512: po * QT + hh * 512 + 512],
                                            bk, AF.Gelu,
                                            scale=sc_sb[:, SC_SH2 + po:SC_SH2 + po + 1])
                    return S.step("scalar", [tok_m], ga)
                eps2 = emit_pass([eps1[-1]], HC, HC, OFF_H2, 4096, list(range(6)), h2_src, h2_out)

                st_toks = {}

                def w3_extra(g, po, slot):
                    g.dma_start(res_b[:, slot * QT:(slot + 1) * QT],
                                src_res_hbm[po * 128:(po + 1) * 128, col0:col0 + QT]
                                ).then_inc(S.dsem["gpsimd"], 16)
                    return 1

                def w3_src(pi, hh):
                    return h_q[:, pi * QT + hh * 512: pi * QT + hh * 512 + 512]

                def w3_out(po, hh, bk, tok_m):
                    par = (po * 2 + hh) % 2

                    def ep(v, po=po, hh=hh, bk=bk, par=par):
                        return v.scalar_tensor_tensor(
                            u1t[:, par * 512:par * 512 + 512],
                            bk, sc_sb[:, SCL_COLS + po:SCL_COLS + po + 1],
                            res_b[:, (po % 2) * QT + hh * 512: (po % 2) * QT + hh * 512 + 512],
                            ALU.mult, ALU.add)
                    tok_e = S.step("vector", [tok_m, tok_s3h, st_toks.get(par)], ep)

                    def ust(g, po=po, hh=hh, par=par):
                        g.dma_start(u_out_hbm[po * 128:(po + 1) * 128,
                                              col0 + hh * 512:col0 + hh * 512 + 512],
                                    u1t[:, par * 512:par * 512 + 512]).then_inc(S.dsem["gpsimd"], 16)
                        return 1
                    tok_us = S.step("gpsimd", [tok_e], ust, dma=True)
                    st_toks[par] = tok_us
                    return tok_us
                eps3 = emit_pass([eps2[-1]], DC, HC, OFF_W3, 4096, [0, 1, 2, 3],
                                 w3_src, w3_out, extra_dma_fn=w3_extra)
                u_done = eps3[-2:]

                if app == 1:
                    def sink_x2(ch, hh, v, t_ap, rb):
                        return v.tensor_tensor(xst[:, hh * 512:hh * 512 + 512], t_ap, rb, ALU.mult)

                    def post_x2(ch, tok_n2, col0=col0):
                        def st(g, ch=ch):
                            g.dma_start(xh2g[ch * 128:(ch + 1) * 128, col0:col0 + QT],
                                        xst[:, 0:QT]).then_inc(S.dsem["gpsimd"], 16)
                            return 1
                        return S.step("gpsimd", [tok_n2], st, dma=True)
                    toks2 = emit_ln2(u_done, u1g, col0, sink_x2, post_ch=post_x2)
                    prev_q = toks2[-1:] + u_done
                else:
                    prev_q = u_done
            return prev_q

        _sc = nc.enter_named_scope("f1", False)
        f1_done = emit_ffn_app(boot + [kprep_done], 1, M_TOK, x_own, u1g)
        nc.leave_named_scope("f1", _sc[0], False)
        dump_hbm("u1", u1g[0:128, 0:2048], 128, 2048)
        dump_hbm("xh2", xh2g[0:128, 0:2048], 128, 2048)

        if phases < 4:
            return finish()
        _sc = nc.enter_named_scope("conv", False)
        boot_c = list(f1_done) + [kprep_done] + [bank_users.get(i) for i in range(6)]
        rd_zt = [None, None]
        rd_hhs = [None, None]
        rd_tpk = [None, None]
        rd_ttsa = [None, None]
        rd_ttsb = [None, None]
        rd_pps = [None, None]
        rd_hps = [None, None]
        rd_gcb = [None, None]
        rd_pa01 = None
        rd_pa23 = None
        rd_pb = None
        rd_pt = None
        rd_ps = None
        last_gst = None
        for grp in range(D // GS):
            d0 = grp * GS
            p = grp % 2
            first = boot_c if grp < 2 else []

            def zload(g, d0=d0, p=p):
                g.dma_start(zt[p][:].rearrange("p (s f) -> p s f", s=GS),
                            xh2g[d0:d0 + GS, :].rearrange("s (p f) -> p s f", p=N2H)
                            ).then_inc(S.dsem["gpsimd"], 16)
                g.dma_start(hhs[p][:, 0:512].rearrange("p (s f) -> p s f", s=GS),
                            hhat[d0:d0 + GS, 0:8192].rearrange("s (p f) -> p s f", p=128)
                            ).then_inc(S.dsem["gpsimd"], 16)
                g.dma_start(hhs[p][:, 512:1024].rearrange("p (s f) -> p s f", s=GS),
                            hhat[d0:d0 + GS, 8192:16384].rearrange("s (p f) -> p s f", p=128)
                            ).then_inc(S.dsem["gpsimd"], 16)
                return 3
            tok_z = S.step("gpsimd", first + [rd_zt[p], rd_hhs[p]], zload, dma=True)

            def cA(t, p=p):
                t.matmul(PA[:, 0:512], c_wa[0:32, :], zt[p][:, 0:512], start=True, stop=True)
                return t.matmul(PA[:, 512:1024], c_wa[0:32, :], zt[p][:, 512:1024],
                                start=True, stop=True)
            tok_a = S.step("tensor", first + [tok_z, rd_pa01], cA)
            rd_zt[p] = tok_a

            def ctw(v, p=p):
                tc = cf(C_TWFC, 128, 64)[:, None, :].broadcast_to((64, GS, 128))
                ts_ = cf(C_TWFS, 128, 64)[:, None, :].broadcast_to((64, GS, 128))
                Gr = PA[0:64, 0:1024].rearrange("p (s f) -> p s f", s=GS)
                Gi = PA[64:128, 0:1024].rearrange("p (s f) -> p s f", s=GS)
                w1 = tw1[p][:].rearrange("p (s f) -> p s f", s=GS)
                w2 = tw2[p][:].rearrange("p (s f) -> p s f", s=GS)
                Tr = tpk[p][0:64, :].rearrange("p (s f) -> p s f", s=GS)
                Ti = tpk[p][64:128, :].rearrange("p (s f) -> p s f", s=GS)
                v.tensor_tensor(w1, Gr, tc, ALU.mult)
                v.tensor_tensor(w2, Gi, ts_, ALU.mult)
                v.tensor_tensor(Tr, w1, w2, ALU.add)
                v.tensor_tensor(w1, Gi, tc, ALU.mult)
                v.tensor_tensor(w2, Gr, ts_, ALU.mult)
                return v.tensor_tensor(Ti, w1, w2, ALU.subtract)
            tok_tw = S.step("vector", [tok_a, rd_tpk[p]], ctw)
            rd_pa01 = tok_tw

            def ctr(t, p=p):
                for s8 in range(GS):
                    ins = t.transpose(PT[:, s8 * 128:(s8 + 1) * 128],
                                      tpk[p][:, s8 * 128:(s8 + 1) * 128], c_id[:])
                return ins
            tok_tr = S.step("tensor", [tok_tw, rd_pt], ctr)
            rd_tpk[p] = tok_tr

            def cev(s, p=p):
                return s.copy(tts_a[p][:, 0:1024], PT[:, 0:1024])
            tok_ev = S.step("scalar", [tok_tr, rd_ttsa[p]], cev)
            rd_pt = tok_ev

            def cB(t, p=p):
                TTr = tts_a[p][:].rearrange("p (s f) -> p s f", s=GS)[:, :, 0:64]
                TTi = tts_a[p][:].rearrange("p (s f) -> p s f", s=GS)[:, :, 64:128]
                t.matmul(PB[:, 0:512], c_ec[:], TTr, start=True, stop=False)
                t.matmul(PB[:, 0:512], c_es[:], TTi, start=False, stop=True)
                t.matmul(PB[:, 512:1024], c_ec[:], TTi, start=True, stop=False)
                return t.matmul(PB[:, 512:1024], c_nes[:], TTr, start=False, stop=True)
            tok_b = S.step("tensor", first + [tok_ev, rd_pb], cB)
            rd_ttsa[p] = tok_b

            def cfq(v, p=p):
                Hr, Hi = hhs[p][:, 0:512], hhs[p][:, 512:1024]
                v.tensor_tensor(fq1[p][:], Hr, PB[:, 0:512], ALU.mult)
                v.tensor_tensor(fq2[p][:], Hi, PB[:, 512:1024], ALU.mult)
                v.tensor_tensor(pps[p][:, 0:512], fq1[p][:], fq2[p][:], ALU.subtract)
                v.tensor_tensor(fq1[p][:], Hr, PB[:, 512:1024], ALU.mult)
                v.tensor_tensor(fq2[p][:], Hi, PB[:, 0:512], ALU.mult)
                return v.tensor_tensor(pps[p][:, 512:1024], fq1[p][:], fq2[p][:], ALU.add)
            tok_f = S.step("vector", [tok_b, rd_pps[p]], cfq)
            rd_pb = tok_f
            rd_hhs[p] = tok_f

            def ciA(t, p=p):
                t.matmul(PA[:, 1024:1536], c_ec[:], pps[p][:, 0:512], start=True, stop=False)
                t.matmul(PA[:, 1024:1536], c_nes[:], pps[p][:, 512:1024], start=False, stop=True)
                t.matmul(PA[:, 1536:2048], c_es[:], pps[p][:, 0:512], start=True, stop=False)
                return t.matmul(PA[:, 1536:2048], c_ec[:], pps[p][:, 512:1024], start=False, stop=True)
            tok_ia = S.step("tensor", first + [tok_f, rd_pa23], ciA)
            rd_pps[p] = tok_ia

            def ctwi(v, p=p):
                tc = cf(C_TWIC, 64)[:, None, :].broadcast_to((128, GS, 64))
                ts_ = cf(C_TWIS, 64)[:, None, :].broadcast_to((128, GS, 64))
                Gr = PA[:, 1024:1536].rearrange("p (s f) -> p s f", s=GS)
                Gi = PA[:, 1536:2048].rearrange("p (s f) -> p s f", s=GS)
                w1 = ti1[p][:].rearrange("p (s f) -> p s f", s=GS)
                w2 = ti2[p][:].rearrange("p (s f) -> p s f", s=GS)
                Hr = hps[p][:].rearrange("p (s f) -> p s f", s=GS)[:, :, 0:64]
                Hi = hps[p][:].rearrange("p (s f) -> p s f", s=GS)[:, :, 64:128]
                v.tensor_tensor(w1, Gr, tc, ALU.mult)
                v.tensor_tensor(w2, Gi, ts_, ALU.mult)
                v.tensor_tensor(Hr, w1, w2, ALU.subtract)
                v.tensor_tensor(w1, Gr, ts_, ALU.mult)
                v.tensor_tensor(w2, Gi, tc, ALU.mult)
                return v.tensor_tensor(Hi, w1, w2, ALU.add)
            tok_twi = S.step("vector", [tok_ia, rd_hps[p]], ctwi)
            rd_pa23 = tok_twi

            def ctr2(t, p=p):
                for s8 in range(GS):
                    ins = t.transpose(PT[:, s8 * 128:(s8 + 1) * 128],
                                      hps[p][:, s8 * 128:(s8 + 1) * 128], c_id[:])
                return ins
            tok_tr2 = S.step("tensor", [tok_twi, rd_pt], ctr2)
            rd_hps[p] = tok_tr2

            def cev2(s, p=p):
                return s.copy(tts_b[p][:, 0:1024], PT[:, 0:1024])
            tok_ev2 = S.step("scalar", [tok_tr2, rd_ttsb[p]], cev2)
            rd_pt = tok_ev2

            def cfin1(t, p=p):
                return t.matmul(PS[0:16, 0:512], c_wb[:], tts_b[p][:, 0:512],
                                start=True, stop=True)
            tok_f1c = S.step("tensor", first + [tok_ev2, rd_ps], cfin1)

            def cge1(s, p=p):
                return s.activation(gcb[p][0:16, 0:512], PS[0:16, 0:512], AF.Gelu)
            tok_g1 = S.step("scalar", [tok_f1c, rd_gcb[p]], cge1)

            def cfin2(t, p=p):
                return t.matmul(PS[0:16, 0:512], c_wb[:], tts_b[p][:, 512:1024],
                                start=True, stop=True)
            tok_f2c = S.step("tensor", [tok_g1], cfin2)
            rd_ttsb[p] = tok_f2c

            def cge2(s, p=p):
                return s.activation(gcb[p][0:16, 512:1024], PS[0:16, 0:512], AF.Gelu)
            tok_g2 = S.step("scalar", [tok_f2c], cge2)
            rd_ps = tok_g2

            def gst(g, d0=d0, p=p):
                g.dma_start(gcg[d0:d0 + GS, :].rearrange("s (p f) -> p s f", p=16),
                            gcb[p][0:16, :].rearrange("p (s f) -> p s f", s=GS)
                            ).then_inc(S.dsem["gpsimd"], 16)
                return 1
            tok_gs = S.step("gpsimd", [tok_g2], gst, dma=True)
            rd_gcb[p] = tok_gs
            last_gst = tok_gs
        nc.leave_named_scope("conv", _sc[0], False)
        conv_done = last_gst
        dump_hbm("gc", gcg[0:128, 0:2048], 128, 2048)

        if phases < 5:
            return finish()

        _sc = nc.enter_named_scope("proj", False)
        def ukeep(g):
            pid = g.partition_id()
            off = nc.s_assert_within(pid % 2, 0, 1, skip_runtime_assert=True)
            u1v = u1g.rearrange("d (kk f) -> d kk f", kk=2)
            g.dma_start(ukg[:, :].rearrange("d f -> d () f")[:, 0:1, :],
                        u1v[:, bass.ts(off, 1), :]).then_inc(S.dsem["gpsimd"], 16)
            return 1
        tok_uk = S.step("gpsimd", list(f1_done), ukeep, dma=True)

        def gcl(g):
            g.dma_start(h_q[:, 0:DC * KEEP].rearrange("p (c f) -> p c f", c=DC),
                        gcg[:, :].rearrange("(c p) f -> p c f", p=128)
                        ).then_inc(S.dsem["gpsimd"], 16)
            return 1
        tok_gc = S.step("gpsimd", [conv_done], gcl, dma=True)

        proj_done = [tok_gc, tok_uk]
        for hf in range(2):
            colh = hf * QT
            ust_toks = {}

            def pj_extra(g, po, slot, colh=colh):
                g.dma_start(res_b[:, slot * QT:slot * QT + QT],
                            ukg[po * 128:(po + 1) * 128, colh:colh + QT]
                            ).then_inc(S.dsem["gpsimd"], 16)
                return 1

            def pj_src(pi, hh, colh=colh):
                return h_q[:, pi * KEEP + colh + hh * 512: pi * KEEP + colh + hh * 512 + 512]

            def pj_out(po, hh, bk, tok_m, colh=colh, ust_toks=ust_toks):
                par = (po * 2 + hh) % 2

                def ep(v, po=po, hh=hh, bk=bk, par=par):
                    return v.scalar_tensor_tensor(
                        u1t[:, par * 512:par * 512 + 512],
                        bk, sc_sb[:, SC_B + po:SC_B + po + 1],
                        res_b[:, (po % 2) * QT + hh * 512: (po % 2) * QT + hh * 512 + 512],
                        ALU.add, ALU.add)
                tok_e = S.step("vector", [tok_m, ust_toks.get(par)], ep)

                def ust(g, po=po, hh=hh, par=par, colh=colh):
                    g.dma_start(u2g[po * 128:(po + 1) * 128,
                                    colh + hh * 512:colh + hh * 512 + 512],
                                u1t[:, par * 512:par * 512 + 512]).then_inc(S.dsem["gpsimd"], 16)
                    return 1
                tok_us = S.step("gpsimd", [tok_e], ust, dma=True)
                ust_toks[par] = tok_us
                return tok_us
            eps = emit_pass(proj_done, DC, DC, 0, 1024, [0, 1, 2, 3], pj_src, pj_out,
                            extra_dma_fn=pj_extra, wsrc=wpb)
            proj_done = eps[-2:]
        nc.leave_named_scope("proj", _sc[0], False)
        dump_hbm("u2", u2g[0:128, 0:2048], 128, 2048)

        if phases < 6:
            return finish()
        _sc = nc.enter_named_scope("f2", False)
        f2_done = emit_ffn_app(proj_done, 2, KEEP, u2g, u3g)
        nc.leave_named_scope("f2", _sc[0], False)
        dump_hbm("u3", u3g[0:128, 0:2048], 128, 2048)
        dump_hbm("xhf2", xh_q[0:128, 0:2048], 128, 2048)
        dump_hbm("hf2", h_q[0:128, 0:2048], 128, 2048)

        if phases < 7:
            return finish()
        _sc = nc.enter_named_scope("oxp", False)
        prev_o = list(f2_done)
        for tb in range(KEEP // 128):
            def old(g, tb=tb):
                g.dma_start(osb[:].rearrange("p (c f) -> p c f", c=DC),
                            u3g[:, tb * 128:(tb + 1) * 128].rearrange("(c p) f -> p c f", p=128)
                            ).then_inc(S.dsem["gpsimd"], 16)
                return 1
            tok_ol = S.step("gpsimd", prev_o, old, dma=True)

            def otr(t, tb=tb):
                for c in range(DC):
                    ins = t.transpose(PA[:, 1024 + c * 128:1152 + c * 128],
                                      osb[:, c * 128:(c + 1) * 128], cf(C_IDF, 128))
                return ins
            tok_tr = S.step("tensor", [tok_ol] + prev_o, otr)

            def oev(s):
                return s.copy(osb[:, 0:1024], PA[:, 1024:2048])
            tok_te = S.step("scalar", [tok_tr], oev)

            def ost(sy, tb=tb):
                sy.dma_start(out[tb * 128:(tb + 1) * 128, :], osb[:, 0:1024]
                             ).then_inc(S.dsem["sync"], 16)
                return 1
            tok_st = S.step("sync", [tok_te], ost, dma=True)
            prev_o = [tok_st, tok_te]
        nc.leave_named_scope("oxp", _sc[0], False)

        return finish()
    return nc, dump_map


_CACHE = {}


def kernel(**inputs):
    x = np.asarray(inputs["L0_attn_x"], np.float32)
    wblob = pack_ffn_weights(inputs)
    wproj = pack_proj_weights(inputs["L0_attn_W"])
    sclv = pack_scales(inputs)
    ssm = pack_ssm(inputs)

    if "prog" not in _CACHE:
        _CACHE["prog"] = build_program()
    nc, dump_map = _CACHE["prog"]

    in_maps = []
    for c in range(NCORE):
        b = c // 2
        cstv, cstbv = host_constants(c)
        xb = np.ascontiguousarray(x[b].T)
        in_maps.append(dict(x_own=xb, wblob=wblob, wproj=wproj, scl=sclv,
                            ssmp=ssm, cst=cstv, cstb=cstbv))
    res = run_bass_kernel_spmd(nc, in_maps, core_ids=list(range(NCORE)))

    outp = np.empty((B, L, D), np.float32)
    for c in range(NCORE):
        b, half = c // 2, c % 2
        outp[b, half * KEEP:(half + 1) * KEEP, :] = res.results[c]["out"]
    return outp
